# version 01
# baseline (speedup 1.0000x reference)
import numpy as np
import concourse.bass as bass
import concourse.mybir as mybir
from concourse.bass_utils import run_bass_kernel_spmd
from concourse.alu_op_type import AluOpType as OP

F32 = mybir.dt.float32
AF = mybir.ActivationFunctionType
AX = mybir.AxisListType.X

D = 1024
HC = 8
S = 8192
TOK = 2048
NKC = 16
LOGK = float(np.log(1152.0))


def ts(i, n=128):
    return slice(i * n, (i + 1) * n)


def build():
    nc = bass.Bass("TRN2", target_bir_lowering=False)
    dt = nc.dram_tensor
    xo = dt("xo", [TOK, D], F32, kind="ExternalInput").ap()
    xf = dt("xf", [S, D], F32, kind="ExternalInput").ap()
    wqt = dt("wqt", [D, D], F32, kind="ExternalInput").ap()
    wvt = dt("wvt", [D, D], F32, kind="ExternalInput").ap()
    amt = dt("amt", [D, D], F32, kind="ExternalInput").ap()
    bq = dt("bq", [D], F32, kind="ExternalInput").ap()
    bvb = dt("bvb", [128, D], F32, kind="ExternalInput").ap()
    rt = dt("rt", [D], F32, kind="ExternalInput").ap()
    gl = dt("gl", [D], F32, kind="ExternalInput").ap()
    gh = dt("gh", [D], F32, kind="ExternalInput").ap()
    w1lt = dt("w1lt", [D, 512], F32, kind="ExternalInput").ap()
    b1l = dt("b1l", [512], F32, kind="ExternalInput").ap()
    w2lt = dt("w2lt", [512, D], F32, kind="ExternalInput").ap()
    b2l = dt("b2l", [D], F32, kind="ExternalInput").ap()
    w1ht = dt("w1ht", [D, 4096], F32, kind="ExternalInput").ap()
    b1h = dt("b1h", [4096], F32, kind="ExternalInput").ap()
    w2ht = dt("w2ht", [4096, D], F32, kind="ExternalInput").ap()
    b2h = dt("b2h", [D], F32, kind="ExternalInput").ap()
    ident = dt("ident", [128, 128], F32, kind="ExternalInput").ap()
    m1 = dt("m1", [128, 128], F32, kind="ExternalInput").ap()
    l64 = dt("l64", [64, 64], F32, kind="ExternalInput").ap()
    onesin = dt("onesin", [128, 128], F32, kind="ExternalInput").ap()
    psel = dt("psel", [128, 128], F32, kind="ExternalInput").ap()
    sdram = dt("sdram", [S], F32, kind="Internal").ap()
    mdram = dt("mdram", [TOK], F32, kind="Internal").ap()
    ndram = dt("ndram", [TOK], F32, kind="Internal").ap()
    out = dt("out", [TOK, D], F32, kind="ExternalOutput").ap()

    ops = []
    G = [0]

    def em(eng, fn, inc=1):
        ops.append((eng, fn, G[0], inc))
        G[0] += inc

    def mm(psum, lhsT, rhs, st, sp):
        em("tensor", lambda l=lhsT, r=rhs, o=psum, a=st, b=sp:
           nc.tensor.matmul(o, l, r, start=a, stop=b))

    def tp(psum, in_, idt):
        em("tensor", lambda o=psum, i=in_, d=idt: nc.tensor.transpose(o, i, d))

    def act(o, i, f, bias=None, scale=None):
        kw = {}
        if bias is not None:
            kw["bias"] = bias
        if scale is not None:
            kw["scale"] = scale
        em("scalar", lambda o=o, i=i, f=f, kw=kw: nc.scalar.activation(o, i, f, **kw))

    def cp(o, i):
        em("scalar", lambda o=o, i=i: nc.scalar.copy(o, i))

    def tt(o, a, b, op):
        em("vector", lambda o=o, a=a, b=b, op=op: nc.vector.tensor_tensor(o, a, b, op))

    def tsc(o, a, s, op):
        em("vector", lambda o=o, a=a, s=s, op=op:
           nc.vector.tensor_scalar(o, a, s, None, op0=op))

    def red(o, i, op):
        em("vector", lambda o=o, i=i, op=op: nc.vector.tensor_reduce(o, i, AX, op))

    def mset(a, v):
        em("vector", lambda a=a, v=v: nc.vector.memset(a, v))

    def recip(o, i):
        em("vector", lambda o=o, i=i: nc.vector.reciprocal(o, i))

    def dma(o, i, eng="sync"):
        em(eng, lambda o=o, i=i: getattr(nc, eng).dma_start(o, i), inc=16)

    from contextlib import ExitStack
    with ExitStack() as stack:
        E = stack.enter_context
        ACC = E(nc.sbuf_tensor("ACC", [128, 16384], F32))
        XNT = E(nc.sbuf_tensor("XNT", [128, 4096], F32))
        QKC = E(nc.sbuf_tensor("QKC", [128, 4096], F32))
        VKC = E(nc.sbuf_tensor("VKC", [128, 4096], F32))
        WRS = E(nc.sbuf_tensor("WRS", [128, 1024], F32))
        SCORE = E(nc.sbuf_tensor("SCORE", [128, 512], F32))
        BB = E(nc.sbuf_tensor("BB", [128, 512], F32))
        HG = E(nc.sbuf_tensor("HG", [128, 512], F32))
        IDT = E(nc.sbuf_tensor("IDT", [128, 128], F32))
        M1T = E(nc.sbuf_tensor("M1T", [128, 128], F32))
        L64T = E(nc.sbuf_tensor("L64T", [64, 64], F32))
        ONE = E(nc.sbuf_tensor("ONE", [128, 128], F32))
        PSL = E(nc.sbuf_tensor("PSL", [128, 128], F32))
        BQ = E(nc.sbuf_tensor("BQ", [128, 8], F32))
        RTK = E(nc.sbuf_tensor("RTK", [128, 8], F32))
        GLT = E(nc.sbuf_tensor("GLT", [128, 8], F32))
        GHT = E(nc.sbuf_tensor("GHT", [128, 8], F32))
        B1L = E(nc.sbuf_tensor("B1L", [128, 4], F32))
        B2L = E(nc.sbuf_tensor("B2L", [128, 8], F32))
        B1H = E(nc.sbuf_tensor("B1H", [128, 32], F32))
        B2H = E(nc.sbuf_tensor("B2H", [128, 8], F32))
        BVB = E(nc.sbuf_tensor("BVB", [128, 1024], F32))
        R512 = E(nc.sbuf_tensor("R512", [1, 2048], F32))
        SCAL = E(nc.sbuf_tensor("SCAL", [1, 16], F32))
        RS = E(nc.sbuf_tensor("RS", [128, 64], F32))
        RB = E(nc.sbuf_tensor("RB", [128, 64], F32))
        RSA = E(nc.sbuf_tensor("RSA", [128, 64], F32))
        RSCR = E(nc.sbuf_tensor("RSCR", [128, 64], F32))
        RE = E(nc.sbuf_tensor("RE", [128, 64], F32))
        RT1 = E(nc.sbuf_tensor("RT1", [128, 64], F32))
        RT2 = E(nc.sbuf_tensor("RT2", [128, 12], F32))
        RSEL = E(nc.sbuf_tensor("RSEL", [128, 64], F32))
        MST = E(nc.sbuf_tensor("MST", [128, 16], F32))
        LST = E(nc.sbuf_tensor("LST", [128, 16], F32))
        AL = E(nc.sbuf_tensor("AL", [128, 4], F32))
        P0 = E(nc.psum_tensor("P0", [128, 512], F32))
        P1b = E(nc.psum_tensor("P1b", [128, 512], F32))
        P2b = E(nc.psum_tensor("P2b", [128, 512], F32))
        P3r = E(nc.psum_tensor("P3r", [128, 512], F32))
        T0 = E(nc.psum_tensor("T0", [128, 128], F32))
        UT = E(nc.sbuf_tensor("UT", [128, 16384], F32))
        AROW = R512
        LROW = UT
        MROW = None
        g = E(nc.semaphore("g"))
        blk = E(nc.Block())
        CB = E(nc.sbuf_tensor("CB", [128, 4], F32))
        r0 = R512[0:1, 0:512]
        r1 = R512[0:1, 512:1024]
        r2 = R512[0:1, 1024:1536]
        r3 = R512[0:1, 1536:2048]
        XT = VKC[:, 0:1024]
        onec = ONE[:, 0:1]
        oner = ONE[0:1, 0:128]

        mset(CB[:, 0:1], 1e-5)
        mset(CB[:, 1:2], 1e-9)
        mset(CB[:, 2:3], LOGK)
        mset(CB[:, 3:4], 1.0)
        dma(IDT[:, :], ident)
        dma(M1T[:, :], m1)
        dma(L64T[:, :], l64)
        dma(ONE[:, :], onesin)
        dma(PSL[:, :], psel)
        dma(BVB[:, :], bvb)
        for i in range(8):
            dma(BQ[:, i:i + 1], bq[ts(i)][:, None])
            dma(RTK[:, i:i + 1], rt[ts(i)][:, None])
            dma(GLT[:, i:i + 1], gl[ts(i)][:, None])
            dma(GHT[:, i:i + 1], gh[ts(i)][:, None])
            dma(B2L[:, i:i + 1], b2l[ts(i)][:, None])
            dma(B2H[:, i:i + 1], b2h[ts(i)][:, None])
        for i in range(4):
            dma(B1L[:, i:i + 1], b1l[ts(i)][:, None])
        for i in range(32):
            dma(B1H[:, i:i + 1], b1h[ts(i)][:, None])

        def bcast(row_ap, width, dest):
            mm(P0[:, 0:width], oner, row_ap, True, True)
            cp(dest[:, 0:width], P0[:, 0:width])

        def ln_chunk(src_hbm, t0, save_norm):
            for tsu in range(4):
                dma(XT, src_hbm[t0 + tsu * 128: t0 + (tsu + 1) * 128, :])
                for hcb in range(HC):
                    tp(T0[:, :], XT[:, ts(hcb)], IDT[:, :])
                    cp(XNT[:, hcb * 512 + tsu * 128: hcb * 512 + tsu * 128 + 128],
                       T0[:, :])
            for hcb in range(HC):
                mm(P3r[0:1, :], onec, XNT[:, ts(hcb, 512)], hcb == 0, hcb == 7)
            act(r0, P3r[0:1, :], AF.Copy, scale=1.0 / 1024.0)
            for hcb in range(HC):
                act(SCORE[:, :], XNT[:, ts(hcb, 512)], AF.Square)
                mm(P3r[0:1, :], onec, SCORE[:, :], hcb == 0, hcb == 7)
            act(r3, r0, AF.Square)
            act(r1, P3r[0:1, :], AF.Copy, scale=1.0 / 1024.0)
            tt(r1, r1, r3, OP.subtract)
            act(r3, r1, AF.Sqrt, bias=CB[0:1, 0:1])
            recip(r2, r3)
            bcast(r0, 512, BB)
            bcast(r2, 512, SCORE)
            for hcb in range(HC):
                tt(XNT[:, ts(hcb, 512)], XNT[:, ts(hcb, 512)], BB[:, :], OP.subtract)
                tt(XNT[:, ts(hcb, 512)], XNT[:, ts(hcb, 512)], SCORE[:, :], OP.mult)
            for hcb in range(HC):
                act(SCORE[:, :], XNT[:, ts(hcb, 512)], AF.Square)
                mm(P3r[0:1, :], onec, SCORE[:, :], hcb == 0, hcb == 7)
            act(r0, P3r[0:1, :], AF.Sqrt)
            if save_norm is not None:
                dma(save_norm, r0, "gpsimd")
            act(r3, r0, AF.Copy, bias=1e-9)
            recip(r1, r3)
            bcast(r1, 512, SCORE)
            return SCORE

        def scale_xnt():
            for hcb in range(HC):
                tt(XNT[:, ts(hcb, 512)], XNT[:, ts(hcb, 512)], SCORE[:, :], OP.mult)

        def qproj():
            for oc in range(HC):
                for hcb in range(HC):
                    dma(WRS[:, ts(hcb)], wqt[ts(hcb), ts(oc)])
                for hcb in range(HC):
                    mm(P0[:, :], WRS[:, ts(hcb)], XNT[:, ts(hcb, 512)],
                       hcb == 0, hcb == 7)
                em("vector", lambda oc=oc: nc.vector.tensor_scalar_add(
                    QKC[:, ts(oc, 512)], P0[:, :], BQ[:, oc:oc + 1]))

        for tc in range(4):
            ln_chunk(xo, tc * 512, ndram[tc * 512:(tc + 1) * 512][None, :])
            scale_xnt()
            qproj()
            for hpc in range(HC):
                for hcb in range(HC):
                    dma(WRS[:, ts(hcb)], amt[ts(hcb), ts(hpc)])
                for hcb in range(HC):
                    mm(P0[:, :], WRS[:, ts(hcb)], QKC[:, ts(hcb, 512)],
                       hcb == 0, hcb == 7)
                cp(UT[:, hpc * 2048 + tc * 512: hpc * 2048 + (tc + 1) * 512],
                   P0[:, :])

        mset(ACC[:, :], 0.0)
        mset(MST[:, :], -60.0)
        mset(LST[:, :], 0.0)
        for kc in range(NKC):
            ln_chunk(xf, kc * 512, None)
            for hcb in range(HC):
                mm(P3r[0:1, :], RTK[:, hcb:hcb + 1], XNT[:, ts(hcb, 512)],
                   hcb == 0, hcb == 7)
            cp(r3, P3r[0:1, :])
            dma(sdram[kc * 512:(kc + 1) * 512][None, :], r3, "gpsimd")
            scale_xnt()
            qproj()
            for oh in range(2):
                for hcb in range(HC):
                    dma(WRS[:, 0:512], wvt[ts(hcb), ts(oh, 512)])
                    for tsu in range(4):
                        mm([P0, P1b, P2b, P3r][tsu][:, :],
                           XNT[:, hcb * 512 + tsu * 128: hcb * 512 + tsu * 128 + 128],
                           WRS[:, 0:512], hcb == 0, hcb == 7)
                for tsu in range(4):
                    tt(VKC[:, tsu * 1024 + oh * 512: tsu * 1024 + (oh + 1) * 512],
                       [P0, P1b, P2b, P3r][tsu][:, :], BVB[:, ts(oh, 512)], OP.add)
            for gq in range(4):
                for qtl in range(4):
                    qt = gq * 4 + qtl
                    for hcb in range(HC):
                        mm(P2b[:, :], UT[:, hcb * 2048 + qt * 128: hcb * 2048 + qt * 128 + 128],
                           QKC[:, ts(hcb, 512)], hcb == 0, hcb == 7)
                    act(SCORE[:, :], P2b[:, :], AF.Copy, scale=1.0 / 32.0)
                    red(AL[:, 0:1], SCORE[:, :], OP.max)
                    tt(AL[:, 1:2], MST[:, qt:qt + 1], AL[:, 0:1], OP.max)
                    tt(AL[:, 2:3], MST[:, qt:qt + 1], AL[:, 1:2], OP.subtract)
                    act(AL[:, 3:4], AL[:, 2:3], AF.Exp)
                    em("vector", lambda qt=qt: nc.vector.tensor_copy(
                        MST[:, qt:qt + 1], AL[:, 1:2]))
                    em("scalar", lambda: nc.scalar.mul(AL[:, 2:3], AL[:, 1:2], -1.0))
                    act(SCORE[:, :], SCORE[:, :], AF.Exp, bias=AL[:, 2:3])
                    red(AL[:, 0:1], SCORE[:, :], OP.add)
                    tt(LST[:, qt:qt + 1], LST[:, qt:qt + 1], AL[:, 3:4], OP.mult)
                    tt(LST[:, qt:qt + 1], LST[:, qt:qt + 1], AL[:, 0:1], OP.add)
                    tp(T0[0:1, 0:128], AL[:, 3:4], IDT[:, :])
                    cp(AROW[0:1, ts(qt)], T0[0:1, 0:128])
                    for j in range(4):
                        tp(T0[:, :], SCORE[:, ts(j)], IDT[:, :])
                        cp(XNT[:, j * 512 + qtl * 128: j * 512 + qtl * 128 + 128],
                           T0[:, :])
                mm(P0[:, :], oner, AROW[0:1, ts(gq, 512)], True, True)
                cp(BB[:, :], P0[:, :])
                for hcb in range(HC):
                    for j in range(4):
                        mm(P1b[:, :], VKC[:, j * 1024 + hcb * 128: j * 1024 + hcb * 128 + 128],
                           XNT[:, ts(j, 512)], j == 0, j == 3)
                    asl = ACC[:, hcb * 2048 + gq * 512: hcb * 2048 + (gq + 1) * 512]
                    tt(SCORE[:, :], asl, BB[:, :], OP.mult)
                    tt(asl, SCORE[:, :], P1b[:, :], OP.add)
        for qt in range(16):
            tp(T0[0:1, 0:128], LST[:, qt:qt + 1], IDT[:, :])
            cp(LROW[0:1, ts(qt)], T0[0:1, 0:128])
        recip(AROW[0:1, 0:2048], LROW[0:1, 0:2048])
        dma(UT[0:1, 2048:4096], ndram[None, :])
        tt(AROW[0:1, 0:2048], AROW[0:1, 0:2048], UT[0:1, 2048:4096], OP.mult)
        for qs in range(4):
            bcast(AROW[0:1, ts(qs, 512)], 512, BB)
            for hcb in range(HC):
                asl = ACC[:, hcb * 2048 + qs * 512: hcb * 2048 + (qs + 1) * 512]
                tt(SCORE[:, :], asl, BB[:, :], OP.mult)
                em("vector", lambda asl=asl: nc.vector.tensor_copy(asl, SCORE[:, :]))

        dma(RS[:, :], sdram.rearrange("(p f) -> p f", p=128))
        em("scalar", lambda: nc.scalar.mul(RB[:, :], RS[:, :], -1.0))
        for it in range(50):
            tt(RT1[:, :], RS[:, :], RB[:, :], OP.add)
            red(RT2[:, 0:1], RT1[:, :], OP.max)
            tp(T0[0:1, 0:128], RT2[:, 0:1], IDT[:, :])
            red(SCAL[0:1, 0:1], T0[0:1, 0:128], OP.max)
            em("scalar", lambda: nc.scalar.mul(SCAL[0:1, 1:2], SCAL[0:1, 0:1], -1.0))
            mm(T0[:, 0:1], oner, SCAL[0:1, 1:2], True, True)
            cp(RT2[:, 1:2], T0[:, 0:1])
            act(RT1[:, :], RT1[:, :], AF.Exp, bias=RT2[:, 1:2])
            red(RT2[:, 0:1], RT1[:, :], OP.add)
            tp(T0[0:1, 0:128], RT2[:, 0:1], IDT[:, :])
            red(SCAL[0:1, 2:3], T0[0:1, 0:128], OP.add)
            act(SCAL[0:1, 3:4], SCAL[0:1, 2:3], AF.Ln)
            tt(SCAL[0:1, 3:4], SCAL[0:1, 3:4], SCAL[0:1, 0:1], OP.add)
            act(SCAL[0:1, 4:5], SCAL[0:1, 3:4], AF.Copy, bias=LOGK, scale=-1.0)
            mm(T0[:, 0:1], oner, SCAL[0:1, 4:5], True, True)
            cp(RT2[:, 2:3], T0[:, 0:1])
            em("vector", lambda: nc.vector.tensor_scalar_add(
                RSA[:, :], RS[:, :], RT2[:, 2:3]))
            act(RB[:, :], RSA[:, :], AF.Relu)
            em("scalar", lambda: nc.scalar.mul(RB[:, :], RB[:, :], -1.0))
        tt(RT1[:, :], RSA[:, :], RB[:, :], OP.add)
        act(RSCR[:, :], RT1[:, :], AF.Exp)
        mset(SCAL[0:1, 5:6], 0.0)
        mset(SCAL[0:1, 6:7], 2.0)
        for it in range(40):
            tt(SCAL[0:1, 7:8], SCAL[0:1, 5:6], SCAL[0:1, 6:7], OP.add)
            em("scalar", lambda: nc.scalar.mul(SCAL[0:1, 7:8], SCAL[0:1, 7:8], 0.5))
            mm(T0[:, 0:1], oner, SCAL[0:1, 7:8], True, True)
            cp(RT2[:, 3:4], T0[:, 0:1])
            tsc(RE[:, :], RSCR[:, :], RT2[:, 3:4], OP.is_ge)
            red(RT2[:, 4:5], RE[:, :], OP.add)
            tp(T0[0:1, 0:128], RT2[:, 4:5], IDT[:, :])
            red(SCAL[0:1, 8:9], T0[0:1, 0:128], OP.add)
            tsc(SCAL[0:1, 9:10], SCAL[0:1, 8:9], 1023.5, OP.is_ge)
            tt(SCAL[0:1, 10:11], SCAL[0:1, 7:8], SCAL[0:1, 5:6], OP.subtract)
            tt(SCAL[0:1, 10:11], SCAL[0:1, 10:11], SCAL[0:1, 9:10], OP.mult)
            tt(SCAL[0:1, 5:6], SCAL[0:1, 5:6], SCAL[0:1, 10:11], OP.add)
            tt(SCAL[0:1, 11:12], SCAL[0:1, 6:7], SCAL[0:1, 7:8], OP.subtract)
            tt(SCAL[0:1, 11:12], SCAL[0:1, 11:12], SCAL[0:1, 9:10], OP.mult)
            tt(SCAL[0:1, 6:7], SCAL[0:1, 7:8], SCAL[0:1, 11:12], OP.add)
        mm(T0[:, 0:1], oner, SCAL[0:1, 6:7], True, True)
        cp(RT2[:, 5:6], T0[:, 0:1])
        mm(T0[:, 0:1], oner, SCAL[0:1, 5:6], True, True)
        cp(RT2[:, 6:7], T0[:, 0:1])
        tsc(RE[:, :], RSCR[:, :], RT2[:, 5:6], OP.is_ge)
        tsc(RT1[:, :], RSCR[:, :], RT2[:, 6:7], OP.is_ge)
        tt(RT1[:, :], RT1[:, :], RE[:, :], OP.subtract)
        red(RT2[:, 4:5], RE[:, :], OP.add)
        tp(T0[0:1, 0:128], RT2[:, 4:5], IDT[:, :])
        red(SCAL[0:1, 12:13], T0[0:1, 0:128], OP.add)
        mm(T0[:, 0:1], oner, SCAL[0:1, 12:13], True, True)
        cp(RT2[:, 7:8], T0[:, 0:1])
        red(RT2[:, 8:9], RT1[:, :], OP.add)
        mm(T0[:, 0:1], M1T[:, :], RT2[:, 8:9], True, True)
        cp(RT2[:, 9:10], T0[:, 0:1])
        tp(T0[0:64, 0:128], RT1[:, 0:64], IDT[:, :])
        cp(HG[0:64, 0:128], T0[0:64, 0:128])
        mm(P1b[0:64, 0:128], L64T[:, :], HG[0:64, 0:128], True, True)
        cp(HG[0:64, 128:256], P1b[0:64, 0:128])
        tp(T0[:, 0:64], HG[0:64, 128:256], IDT[0:64, 0:64])
        cp(RSA[:, :], T0[:, 0:64])
        em("vector", lambda: nc.vector.tensor_scalar_add(RSA[:, :], RSA[:, :], RT2[:, 9:10]))
        em("vector", lambda: nc.vector.tensor_scalar_add(RSA[:, :], RSA[:, :], RT2[:, 7:8]))
        tsc(RB[:, :], RSA[:, :], 1023.5, OP.is_le)
        tt(RT1[:, :], RT1[:, :], RB[:, :], OP.mult)
        tt(RSEL[:, :], RE[:, :], RT1[:, :], OP.add)
        act(RB[:, :], RSCR[:, :], AF.Copy, bias=1.0, scale=-1.0)
        tt(RB[:, :], RB[:, :], RSCR[:, :], OP.add)
        tt(RSEL[:, :], RSEL[:, :], RB[:, :], OP.mult)
        mm(T0[:, 0:64], PSL[:, :], RSEL[:, :], True, True)
        cp(RT1[:, :], T0[:, 0:64])
        dma(mdram.rearrange("(p f) -> p f", p=32), RT1[0:32, :], "gpsimd")
        dma(UT[0:1, 2048:4096], mdram[None, :])

        for tc in range(4):
            ln_chunk(xo, tc * 512, None)
            em("vector", lambda: nc.vector.tensor_scalar_max(r3, r0, 1e-12))
            recip(r1, r3)
            em("scalar", lambda: nc.scalar.mul(r1, r1, 32.0))
            bcast(r1, 512, BB)
            for hcb in range(HC):
                tt(XNT[:, ts(hcb, 512)], XNT[:, ts(hcb, 512)], BB[:, :], OP.mult)
            for hcb in range(HC):
                tsc(QKC[:, ts(hcb, 512)], XNT[:, ts(hcb, 512)],
                    GLT[:, hcb:hcb + 1], OP.mult)
            for dl in range(4):
                for hcb in range(HC):
                    dma(WRS[:, ts(hcb)], w1lt[ts(hcb), ts(dl)])
                for hcb in range(HC):
                    mm(P0[:, :], WRS[:, ts(hcb)], QKC[:, ts(hcb, 512)],
                       hcb == 0, hcb == 7)
                act(VKC[:, ts(dl, 512)], P0[:, :], AF.Gelu, bias=B1L[:, dl:dl + 1])
            for oc in range(HC):
                for dl in range(4):
                    dma(WRS[:, ts(dl)], w2lt[ts(dl), ts(oc)])
                for dl in range(4):
                    mm(P0[:, :], WRS[:, ts(dl)], VKC[:, ts(dl, 512)],
                       dl == 0, dl == 3)
                em("vector", lambda oc=oc: nc.vector.tensor_scalar_add(
                    SCORE[:, :], P0[:, :], B2L[:, oc:oc + 1]))
                asl = ACC[:, oc * 2048 + tc * 512: oc * 2048 + (tc + 1) * 512]
                tt(asl, asl, SCORE[:, :], OP.add)
            for hcb in range(HC):
                tsc(QKC[:, ts(hcb, 512)], XNT[:, ts(hcb, 512)],
                    GHT[:, hcb:hcb + 1], OP.mult)
            bcast(UT[0:1, 2048 + tc * 512: 2048 + (tc + 1) * 512], 512, BB)
            for dh in range(32):
                for hcb in range(HC):
                    dma(WRS[:, ts(hcb)], w1ht[ts(hcb), ts(dh)])
                for hcb in range(HC):
                    mm(P0[:, :], WRS[:, ts(hcb)], QKC[:, ts(hcb, 512)],
                       hcb == 0, hcb == 7)
                act(HG[:, :], P0[:, :], AF.Gelu, bias=B1H[:, dh:dh + 1])
                for oc in range(HC):
                    dma(WRS[:, ts(0)], w2ht[ts(dh), ts(oc)])
                    mm(P1b[:, :], WRS[:, ts(0)], HG[:, :], True, True)
                    tt(SCORE[:, :], P1b[:, :], BB[:, :], OP.mult)
                    asl = ACC[:, oc * 2048 + tc * 512: oc * 2048 + (tc + 1) * 512]
                    tt(asl, asl, SCORE[:, :], OP.add)
            for oc in range(HC):
                tsc(SCORE[:, :], BB[:, :], B2H[:, oc:oc + 1], OP.mult)
                asl = ACC[:, oc * 2048 + tc * 512: oc * 2048 + (tc + 1) * 512]
                tt(asl, asl, SCORE[:, :], OP.add)

        for tsu in range(16):
            t0 = tsu * 128
            for oc in range(HC):
                tp(T0[:, :], ACC[:, oc * 2048 + t0: oc * 2048 + t0 + 128], IDT[:, :])
                cp(VKC[:, ts(oc)], T0[:, :])
            dma(XNT[:, 0:1024], xo[t0:t0 + 128, :])
            tt(VKC[:, 0:1024], VKC[:, 0:1024], XNT[:, 0:1024], OP.add)
            dma(out[t0:t0 + 128, :], VKC[:, 0:1024], "gpsimd")

        GF = G[0]

        def replay(name, eng):
            for (e, fn, wait, inc) in ops:
                if e != name:
                    continue
                eng.wait_ge(g, wait)
                inst = fn()
                inst.then_inc(g, inc)
            eng.wait_ge(g, GF)

        @blk.tensor
        def _(eng):
            replay("tensor", eng)

        @blk.vector
        def _(eng):
            replay("vector", eng)

        @blk.scalar
        def _(eng):
            replay("scalar", eng)

        @blk.sync
        def _(eng):
            replay("sync", eng)

        @blk.gpsimd
        def _(eng):
            replay("gpsimd", eng)

    return nc


def kernel(**inputs):
    f = lambda k: np.ascontiguousarray(np.asarray(inputs[k], dtype=np.float32))
    x = f("L0_attn_hidden_states")
    Wq = f("L0_attn_Wq"); bq = f("L0_attn_bq")
    Wv = f("L0_attn_Wv"); bv = f("L0_attn_bv")
    A = f("L0_attn_anomaly_matrix")
    rt = f("L0_ffn_routing_token")
    gl = f("L0_ffn_gamma_l"); gh = f("L0_ffn_gamma_h")
    w1l = f("L0_ffn_w1_l"); b1l = f("L0_ffn_b1_l")
    w2l = f("L0_ffn_w2_l"); b2l = f("L0_ffn_b2_l")
    w1h = f("L0_ffn_w1_h"); b1h = f("L0_ffn_b1_h")
    w2h = f("L0_ffn_w2_h"); b2h = f("L0_ffn_b2_h")

    C = np.ascontiguousarray
    common = dict(
        wqt=C(Wq.T), wvt=C(Wv.T), amt=C(A.T), bq=bq,
        bvb=C(np.broadcast_to(bv, (128, 1024))), rt=rt, gl=gl, gh=gh,
        w1lt=C(w1l.T), b1l=b1l, w2lt=C(w2l.T), b2l=b2l,
        w1ht=C(w1h.T), b1h=b1h, w2ht=C(w2h.T), b2h=b2h,
        ident=np.eye(128, dtype=np.float32),
        m1=np.fromfunction(lambda i, j: (i < j).astype(np.float32), (128, 128)),
        l64=np.fromfunction(lambda i, j: (i < j).astype(np.float32), (64, 64)),
        onesin=np.ones((128, 128), dtype=np.float32),
    )
    in_maps = []
    for c in range(8):
        B, qi = c // 4, c % 4
        p0 = qi * 32
        psel = np.zeros((128, 128), dtype=np.float32)
        for p in range(32):
            psel[p0 + p, p] = 1.0
        m = dict(common)
        m["xo"] = C(x[B, qi * 2048:(qi + 1) * 2048])
        m["xf"] = C(x[B])
        m["psel"] = psel
        in_maps.append(m)

    nc = build()
    res = run_bass_kernel_spmd(nc, in_maps, core_ids=list(range(8)))
    outp = np.empty((2, 8192, 1024), dtype=np.float32)
    for c in range(8):
        B, qi = c // 4, c % 4
        outp[B, qi * 2048:(qi + 1) * 2048] = res.results[c]["out"]
    return outp
